# version 12
# baseline (speedup 1.0000x reference)
import numpy as np
import ml_dtypes
import concourse.bass as bass
import concourse.mybir as mybir
from concourse.ap import AP
from concourse.bass_utils import run_bass_kernel_spmd

F32 = mybir.dt.float32
F16 = mybir.dt.float16
AF = mybir.ActivationFunctionType
OP = mybir.AluOpType
AX = mybir.AxisListType

B, L, D = 8, 4096, 1024
NBLK = 32
RUN = 38
NF = 128
SEGM = 1024
SEGD = 384
WFM = 1024
WFD = 384
WFRM = 896
WFRD = 256
GM = 4
GD = 8
EPS = 1e-5
NCHUNK = 4
DEBUG_DUMPS = False


NQ = 4


def _groups():
    gs = []
    for q in range(NQ):
        dfq = [r for r in range(32 * q, 32 * q + 32) if (r % 8) < 4]
        maq = [r for r in range(32 * q, 32 * q + 32) if (r % 8) >= 4]
        for i in range(0, 16, GD):
            gs.append(("d", dfq[i:i + GD]))
        for i in range(0, 16, GM):
            gs.append(("m", maq[i:i + GM]))
    return gs


GROUPS = _groups()
NGRP = len(GROUPS)
PROC = [r for _, lst in GROUPS for r in lst]
NPAIR = NF // 2
MA_ORDER = [r for r in range(NF) if (r % 8) >= 4]
DF_ORDER = [r for r in range(NF) if (r % 8) < 4]
MA_POS = {r: i for i, r in enumerate(MA_ORDER)}
DF_POS = {r: i for i, r in enumerate(DF_ORDER)}
GCUM = np.cumsum([len(lst) for _, lst in GROUPS]).tolist()
for _p in range(NPAIR):
    assert PROC[2 * _p + 1] == PROC[2 * _p] + 1


def build_nc():
    nc = bass.Bass("TRN2", target_bir_lowering=False, detect_race_conditions=False)
    u_hbm = nc.dram_tensor("u", [L, D], F32, kind="ExternalInput").ap()
    taps_ma = nc.dram_tensor("taps_ma", [64 * SEGM], F32, kind="ExternalInput").ap()
    taps_df = nc.dram_tensor("taps_df", [64 * SEGD], F32, kind="ExternalInput").ap()
    atR_h = nc.dram_tensor("atR", [4, 32, 32], F32, kind="ExternalInput").ap()
    atI_h = nc.dram_tensor("atI", [4, 32, 32], F32, kind="ExternalInput").ap()
    btR_h = nc.dram_tensor("btR", [4, 16, 16], F32, kind="ExternalInput").ap()
    btI_h = nc.dram_tensor("btI", [4, 16, 16], F32, kind="ExternalInput").ap()
    bias_h = nc.dram_tensor("biascat", [1, D], F32, kind="ExternalInput").ap()
    selq_h = nc.dram_tensor("selq", [32, 4 * 128], F16, kind="ExternalInput").ap()
    selg_h = nc.dram_tensor("selg", [16, 128], F16, kind="ExternalInput").ap()
    sid_h = nc.dram_tensor("sid", [128, 128], F16, kind="ExternalInput").ap()
    ones_h = nc.dram_tensor("ones", [1, 128], F16, kind="ExternalInput").ap()
    out_hbm = nc.dram_tensor("out", [L, D], F32, kind="ExternalOutput").ap()
    tma16 = nc.dram_tensor("tma16", [64 * SEGM], F16)
    tdf16 = nc.dram_tensor("tdf16", [64 * SEGD], F16)
    if DEBUG_DUMPS:
        d_ln1h = nc.dram_tensor("d_ln1h", [128, 8 + 512 * RUN], F16, kind="ExternalOutput").ap()
        d_kt0 = nc.dram_tensor("d_kt0", [128, GM * WFM], F16, kind="ExternalOutput").ap()
        d_stats = nc.dram_tensor("d_stats", [128, 6 * NBLK], F32, kind="ExternalOutput").ap()
        d_wbig = nc.dram_tensor("d_wbig", [128, 8 * 1024], F16, kind="ExternalOutput").ap()
        d_x1 = nc.dram_tensor("d_x1", [128, NBLK * D], F32, kind="ExternalOutput").ap()
        d_sst = nc.dram_tensor("d_sst", [128, NBLK * 12], F32, kind="ExternalOutput").ap()
        d_pp = nc.dram_tensor("d_pp", [128, 160], F32, kind="ExternalOutput").ap()

    sbt = nc.sbuf_tensor
    pst = nc.psum_tensor
    from contextlib import ExitStack
    with ExitStack() as stack:
        en = stack.enter_context
        ux = en(sbt("ux", [128, NBLK * D], F32))
        sst = en(sbt("sst", [128, NBLK * 12], F32))
        inv1 = en(sbt("inv1", [128, NBLK], F32))
        nb1 = en(sbt("nb1", [128, NBLK], F32))
        inv2 = en(sbt("inv2", [128, NBLK], F32))
        nb2 = en(sbt("nb2", [128, NBLK], F32))
        ppsc = en(sbt("ppsc", [128, 240], F32))
        satR = en(sbt("satR", [32, 128], F16))
        satI = en(sbt("satI", [32, 128], F16))
        sbtR = en(sbt("sbtR", [16, 64], F16))
        sbtI = en(sbt("sbtI", [16, 64], F16))
        sselq = en(sbt("sselq", [32, 4 * 128], F16))
        sselg = en(sbt("sselg", [16, 128], F16))
        ssid = en(sbt("ssid", [128, 128], F16))
        sones = en(sbt("sones", [1, 128], F16))
        sbias = en(sbt("sbias", [1, D], F16))
        (s_c16, s_cst, s_gld, s_ppv, s_pps, s_ppf, s_ln1,
         s_msz, s_cmm, s_xev, s_wmm, s_wf2, s_wfm, s_wch, s_ln2, s_tmm,
         s_tev, s_fmm, s_fev) = [
            en(nc.semaphore(n)) for n in
            ["s_c16", "s_cst", "s_gld", "s_ppv",
             "s_pps", "s_ppf", "s_ln1", "s_msz", "s_cmm", "s_xev", "s_wmm",
             "s_wf2", "s_wfm", "s_wch", "s_ln2", "s_tmm", "s_tev", "s_fmm",
             "s_fev"]]
        s_ul = [en(nc.semaphore(f"s_ul{k}")) for k in range(8)]
        s_ulo = [en(nc.semaphore(f"s_ulo{k}")) for k in range(8)]
        s_bcg = [en(nc.semaphore(f"s_bc{g}")) for g in range(NGRP)]
        s_skg = [en(nc.semaphore(f"s_sk{g}")) for g in range(NGRP)]
        s_st = [en(nc.semaphore(f"s_st{k}")) for k in range(NBLK // 2)]
        pbld = en(pst("pbld", [128, 1024], F32))
        with nc.Block() as blk:
            with ExitStack() as cstack:
                cen = cstack.enter_context
                ln1q0 = cen(sbt("ln1q0", [128, 38 * 256], F16))
                ln1q1 = cen(sbt("ln1q1", [128, 38 * 256], F16))
                rep0 = cen(sbt("rep0", [128, GM * SEGM], F16))
                rep1 = cen(sbt("rep1", [128, GM * SEGM], F16))
                kt0 = cen(sbt("kt0", [128, GM * WFM], F16))
                kt1 = cen(sbt("kt1", [128, GM * WFM], F16))
                pc0 = cen(pst("pc0", [128, 1024], F32))
                pc1 = cen(pst("pc1", [128, 1024], F32))
                pc2 = cen(pst("pc2", [128, 1024], F32))
                rep = [rep0, rep1]
                ln1q = [ln1q0, ln1q1]
                kt = [kt0, kt1]
                pconv = [pc0, pc1, pc2]

                def flat(t, off, dims):
                    return AP(t[:].tensor, off, dims)

                def ginfo(gi):
                    kind, lst = GROUPS[gi]
                    if kind == "d":
                        return kind, lst, SEGD, WFD, WFRD, tdf16, DF_POS[lst[0]]
                    return kind, lst, SEGM, WFM, WFRM, tma16, MA_POS[lst[0]]

                @blk.gpsimd
                def _(g):
                    for j in range(1, NBLK, 2):
                        g.dma_start(out=ux[:, j * D:(j + 1) * D],
                                    in_=u_hbm[128 * j:128 * (j + 1), :]).then_inc(s_ulo[j // 4], 16)
                    g.dma_start(out=tma16.ap(), in_=taps_ma).then_inc(s_c16, 16)
                    g.dma_start(out=tdf16.ap(), in_=taps_df).then_inc(s_c16, 16)
                    g.dma_start(out=satR[:], in_=AP(atR_h.tensor, 0, [[32, 32], [1024, 4], [1, 32]])).then_inc(s_gld, 16)
                    g.dma_start(out=satI[:], in_=AP(atI_h.tensor, 0, [[32, 32], [1024, 4], [1, 32]])).then_inc(s_gld, 16)
                    g.dma_start(out=sbtR[:], in_=AP(btR_h.tensor, 0, [[16, 16], [256, 4], [1, 16]])).then_inc(s_gld, 16)
                    g.dma_start(out=sbtI[:], in_=AP(btI_h.tensor, 0, [[16, 16], [256, 4], [1, 16]])).then_inc(s_gld, 16)
                    g.dma_start(out=sbias[:], in_=bias_h).then_inc(s_gld, 16)
                    g.wait_ge(s_c16, 32)
                    for gi in range(NGRP):
                        kind, lst, seg, wf, wfr, src16, p0 = ginfo(gi)
                        if gi == 2:
                            for k in range(8):
                                g.wait_ge(s_ul[k], 32)
                                g.wait_ge(s_ulo[k], 32)
                        if gi >= 2:
                            g.wait_ge(s_skg[gi - 2], 16)
                        n = len(lst) * seg
                        g.dma_start(out=rep[gi % 2][:, 0:n],
                                    in_=src16.ap()[None, p0 * seg: p0 * seg + n]
                                    .partition_broadcast(128)).then_inc(s_bcg[gi], 16)
                    if DEBUG_DUMPS:
                        g.wait_ge(s_ln1, 64)
                        g.dma_start(out=d_stats[:, 0:32], in_=inv1[:]).then_inc(s_c16, 16)
                        g.dma_start(out=d_stats[:, 32:64], in_=nb1[:]).then_inc(s_c16, 16)
                        g.dma_start(out=d_sst, in_=sst[:]).then_inc(s_c16, 16)
                        g.dma_start(out=d_pp, in_=ppsc[:]).then_inc(s_c16, 16)
                        g.wait_ge(s_xev, NPAIR)
                        g.dma_start(out=d_ln1h, in_=ln1h[:]).then_inc(s_c16, 16)
                        g.dma_start(out=d_kt0, in_=kt[0][:]).then_inc(s_c16, 16)
                        g.dma_start(out=d_x1, in_=ux[:]).then_inc(s_c16, 16)

                @blk.sync
                def _(sync):
                    sync.dma_start(out=sselq[:], in_=selq_h).then_inc(s_cst, 16)
                    sync.dma_start(out=sselg[:], in_=selg_h).then_inc(s_cst, 16)
                    sync.dma_start(out=ssid[:], in_=sid_h).then_inc(s_cst, 16)
                    sync.dma_start(out=sones[:], in_=ones_h).then_inc(s_cst, 16)
                    for j in range(0, NBLK, 2):
                        sync.dma_start(out=ux[:, j * D:(j + 1) * D],
                                       in_=u_hbm[128 * j:128 * (j + 1), :]).then_inc(s_ul[j // 4], 16)
                    for gi in range(NGRP):
                        kind, lst, seg, wf, wfr, src16, p0 = ginfo(gi)
                        gsz = len(lst)
                        if gi >= 2:
                            sync.wait_ge(s_cmm, GCUM[gi - 2])
                        sync.wait_ge(s_bcg[gi], 16)
                        src = flat(rep[gi % 2], 127, [[GM * SEGM - 1, 128], [seg, gsz], [1, wfr]])
                        dst = flat(kt[gi % 2], 0, [[GM * WFM, 128], [wf, gsz], [1, wfr]])
                        sync.dma_start(out=dst, in_=src).then_inc(s_skg[gi], 16)


                def emit_postAn(v, jb, nblk, pidx, seq):
                    base = jb * 12
                    meanv = flat(sst, base + 1, [[NBLK * 12, 128], [12, nblk], [3, 4]])
                    m2v = flat(sst, base + 2, [[NBLK * 12, 128], [12, nblk], [3, 4]])
                    q1 = ppsc[:, 64:64 + nblk]
                    q2 = ppsc[:, 80:80 + nblk]
                    mean = ppsc[:, 96:96 + nblk]
                    sq8 = ppsc[:, 112:112 + nblk]
                    q3 = ppsc[:, 128:128 + nblk]
                    q2e = ppsc[:, 224:224 + nblk]
                    msq = flat(ppsc, 144, [[240, 128], [4, nblk], [1, 4]])
                    ve = ppsc[:, pidx * 16: pidx * 16 + nblk]
                    v.drain()
                    v.tensor_reduce(q1, meanv, AX.X, OP.add)
                    v.tensor_tensor(msq, meanv, meanv, OP.mult)
                    v.tensor_reduce(q2, m2v, AX.X, OP.add)
                    v.drain()
                    v.tensor_reduce(q3, msq, AX.X, OP.add)
                    v.tensor_scalar(mean, q1, 0.25, None, OP.mult)
                    v.tensor_scalar(q2e, q2, 1.0, 1024.0 * EPS, OP.mult, OP.add)
                    v.drain()
                    v.tensor_tensor(sq8, mean, mean, OP.mult)
                    v.scalar_tensor_tensor(ve, q3, 256.0, q2e, OP.mult, OP.add)
                    v.drain()
                    v.scalar_tensor_tensor(ve, ve, 1.0 / 1024.0, sq8, OP.mult, OP.subtract)
                    v.drain().then_inc(s_ppv, 1)

                def emit_lnexpn(s, pidx, seq, invT, jb, nblk):
                    s.wait_ge(s_ppv, seq + 1)
                    lnt = ppsc[:, 208:208 + nblk]
                    s.activation(lnt, ppsc[:, pidx * 16: pidx * 16 + nblk], AF.Ln)
                    s.activation(invT[:, jb: jb + nblk], lnt, AF.Exp, scale=-0.5)
                    s.drain().then_inc(s_pps, 1)

                def emit_postBn(v, jb, nblk, invT, nbT, seq):
                    mean = ppsc[:, 96:96 + nblk]
                    v.wait_ge(s_pps, seq + 1)
                    v.scalar_tensor_tensor(nbT[:, jb: jb + nblk], mean, -1.0,
                                           invT[:, jb: jb + nblk],
                                           OP.mult, OP.mult)
                    v.drain().then_inc(s_ppf, 1)

                def emit_postA(v, c16, pidx):
                    base = c16 * 16 * 12
                    meanv = flat(sst, base + 1, [[NBLK * 12, 128], [12, 16], [3, 4]])
                    m2v = flat(sst, base + 2, [[NBLK * 12, 128], [12, 16], [3, 4]])
                    q1 = ppsc[:, 64:80]
                    q2 = ppsc[:, 80:96]
                    mean = ppsc[:, 96:112]
                    sq8 = ppsc[:, 112:128]
                    q3 = ppsc[:, 128:144]
                    q2e = ppsc[:, 224:240]
                    msq = flat(ppsc, 144, [[240, 128], [4, 16], [1, 4]])
                    ve = ppsc[:, pidx * 16: pidx * 16 + 16]
                    v.drain()
                    v.tensor_reduce(q1, meanv, AX.X, OP.add)
                    v.tensor_tensor(msq, meanv, meanv, OP.mult)
                    v.tensor_reduce(q2, m2v, AX.X, OP.add)
                    v.drain()
                    v.tensor_reduce(q3, msq, AX.X, OP.add)
                    v.tensor_scalar(mean, q1, 0.25, None, OP.mult)
                    v.tensor_scalar(q2e, q2, 1.0, 1024.0 * EPS, OP.mult, OP.add)
                    v.drain()
                    v.tensor_tensor(sq8, mean, mean, OP.mult)
                    v.scalar_tensor_tensor(ve, q3, 256.0, q2e, OP.mult, OP.add)
                    v.drain()
                    v.scalar_tensor_tensor(ve, ve, 1.0 / 1024.0, sq8, OP.mult, OP.subtract)
                    v.drain().then_inc(s_ppv, 1)

                def emit_lnexp(s, pidx, invT, c16):
                    s.wait_ge(s_ppv, pidx + 1)
                    lnt = ppsc[:, 208:224]
                    s.activation(lnt, ppsc[:, pidx * 16: pidx * 16 + 16], AF.Ln)
                    s.activation(invT[:, c16 * 16: c16 * 16 + 16], lnt, AF.Exp,
                                 scale=-0.5)
                    s.drain().then_inc(s_pps, 1)

                def emit_postB(v, c16, invT, nbT, pidx):
                    mean = ppsc[:, 96:112]
                    v.wait_ge(s_pps, pidx + 1)
                    v.scalar_tensor_tensor(nbT[:, c16 * 16: c16 * 16 + 16], mean, -1.0,
                                           invT[:, c16 * 16: c16 * 16 + 16],
                                           OP.mult, OP.mult)
                    v.drain().then_inc(s_ppf, 1)

                @blk.vector
                def _(v):
                    v.memset(ln1q0[:, 0:6 * 256], 0.0)
                    v.memset(ln1q1[:, 0:6 * 256], 0.0)
                    v.drain().then_inc(s_msz, 1)
                    for j in range(NBLK):
                        if j % 4 == 0:
                            v.wait_ge(s_ul[j // 4], 32)
                            v.wait_ge(s_ulo[j // 4], 32)
                        v.bn_stats(sst[:, j * 12: j * 12 + 6], ux[:, j * D: j * D + 512])
                        v.bn_stats(sst[:, j * 12 + 6: j * 12 + 12], ux[:, j * D + 512: j * D + 1024])
                        if j % 16 == 15:
                            c16 = j // 16
                            emit_postA(v, c16, c16)
                            emit_postB(v, c16, inv1, nb1, c16)
                    for p in range(NPAIR):
                        v.wait_ge(s_cmm, 2 * (p + 1))
                        r0 = PROC[2 * p]
                        psrc = flat(pconv[p % 3], 0, [[1024, 128], [512, 2], [8, 32], [1, 8]])
                        udst = flat(ux, 8 * r0, [[NBLK * D, 128], [8, 2], [D, 32], [1, 8]])
                        v.tensor_tensor(udst, psrc, udst, OP.add)
                        if p % 2 == 1:
                            v.drain().then_inc(s_xev, 2)

                @blk.scalar
                def _(s):
                    for q in range(NQ):
                        for j in range(NBLK):
                            if q == 0 and j % 16 == 0:
                                emit_lnexp(s, j // 16, inv1, j // 16)
                            if q >= 2 and j == 0:
                                s.wait_ge(s_cmm, 32 * (q - 1))
                            if q == 0:
                                s.wait_ge(s_ppf, j // 16 + 1)
                            if q == 0 and j == 0:
                                s.wait_ge(s_msz, 1)
                            dst = ln1q[q % 2][:, (6 + j) * 256:(7 + j) * 256]
                            s.activation(dst, ux[:, j * D + 256 * q: j * D + 256 * q + 256],
                                         AF.Identity, bias=nb1[:, j:j + 1],
                                         scale=inv1[:, j:j + 1])
                            if j % 16 == 15:
                                s.drain().then_inc(s_ln1, 16)

                @blk.tensor
                def _(t):
                    fidx = 0
                    for gi in range(NGRP):
                        kind, lst, seg, wf, wfr, src16, p0 = ginfo(gi)
                        q = gi // 6
                        t.wait_ge(s_skg[gi], 16)
                        t.wait_ge(s_ln1, 32 * (q + 1))
                        for sl, r in enumerate(lst):
                            pair, sub = fidx // 2, fidx % 2
                            if pair >= 3 and sub == 0:
                                t.wait_ge(s_xev, pair - 2)
                            nd = 2 if kind == "d" else 7
                            pt = pconv[pair % 3]
                            for dlt in range(nd):
                                rhs = flat(ln1q[q % 2], (6 - dlt) * 256 + (8 * r - 256 * q),
                                           [[38 * 256, 128], [256, 32], [1, 8]])
                                ins = t.matmul(
                                    pt[:, sub * 512: sub * 512 + 256],
                                    flat(kt[gi % 2], sl * wf + 128 * dlt,
                                         [[GM * WFM, 128], [1, 128]]),
                                    rhs, start=(dlt == 0), stop=(dlt == nd - 1))
                            ins.then_inc(s_cmm, 1)
                            fidx += 1

            with ExitStack() as fstack:
                fen = fstack.enter_context
                wbig = fen(sbt("wbig", [128, 8 * 1024], F16))
                f2all = fen(sbt("f2all", [128, 8 * 512], F16))
                wacc = fen(sbt("wacc", [128, 1024], F32))
                pscr = fen(sbt("pscr", [128, 1024], F32))
                ln2b = fen(sbt("ln2b", [128, 2 * 1024], F16))
                ln2t = fen(sbt("ln2t", [128, 2 * 1024], F16))
                outsb = fen(sbt("outsb", [128, 2 * 1024], F32))
                ptr0 = fen(pst("ptr0", [128, 1024], F32))
                ptr1 = fen(pst("ptr1", [128, 1024], F32))
                po0 = fen(pst("po0", [128, 1024], F32))
                po1 = pbld
                ptr = [ptr0, ptr1]
                po = [po0, po1]

                def flat2(t, off, dims):
                    return AP(t[:].tensor, off, dims)

                CH_ORDER = [0, 4, 1, 5, 2, 6, 3, 7]
                CH_GATE = {c: (c % 4) + 1 for c in range(8)}

                @blk.tensor
                def _(t):
                    def warm(n, dst):
                        for _ in range(n):
                            t.matmul(dst[:, 0:128], ssid[:], ssid[:],
                                     start=True, stop=True)

                    warm(24, pbld)
                    t.wait_ge(s_xev, NPAIR)
                    t.wait_ge(s_cst, 64)
                    t.wait_ge(s_gld, 80)
                    k = 0
                    for c in range(4):
                        for bt_ in (sbtR, sbtI):
                            if k >= 2:
                                t.wait_ge(s_wf2, k - 1)
                            rhs = flat2(bt_, c * 16, [[64, 16], [0, 32], [1, 16]])
                            t.matmul(po1[:, (k % 2) * 512:(k % 2) * 512 + 512],
                                     sselg[:], rhs, start=True, stop=True).then_inc(s_wmm, 1)
                            k += 1
                    warm(12, po0)

                    def emit_f1_batch(q):
                        kk = 8 * q
                        for c in range(4):
                            for at_ in (satR, satI):
                                m = kk + 2 * c + (0 if at_ is satR else 1)
                                if q == 0 and m == 0:
                                    t.wait_ge(s_wf2, 8)
                                if m >= 2:
                                    t.wait_ge(s_wfm, m - 1)
                                rhs = flat2(at_, c * 32, [[128, 32], [1, 32], [0, 16]])
                                t.matmul(po1[:, (m % 2) * 512:(m % 2) * 512 + 512],
                                         sselq[:, q * 128:(q + 1) * 128], rhs,
                                         start=True, stop=True).then_inc(s_wmm, 1)

                    def emit_block_mms(i, chunks, first, last):
                        for c in chunks:
                            t.wait_ge(s_wch, CH_GATE[c])
                            for hh in range(2):
                                t.matmul(po[i % 2][:, hh * 512: hh * 512 + 512],
                                         ln2t[:, (i % 2) * 1024 + c * 128:
                                              (i % 2) * 1024 + c * 128 + 128],
                                         wbig[:, c * 1024 + hh * 512: c * 1024 + hh * 512 + 512],
                                         start=(c == CH_ORDER[0] and first),
                                         stop=False)
                        if last:
                            for hh in range(2):
                                ins = t.matmul(po[i % 2][:, hh * 512: hh * 512 + 512],
                                               sones[:], sbias[:, hh * 512: hh * 512 + 512],
                                               start=False, stop=True)
                            ins.then_inc(s_fmm, 1)

                    def emit_transposes(i):
                        t.wait_ge(s_ln2, i + 1)
                        if i >= 2:
                            t.wait_ge(s_tev, i - 1)
                        for c in range(8):
                            ins = t.matmul(ptr[i % 2][:, c * 128:(c + 1) * 128],
                                           ln2b[:, (i % 2) * 1024 + c * 128:
                                                (i % 2) * 1024 + c * 128 + 128],
                                           ssid[:], start=True, stop=True)
                        ins.then_inc(s_tmm, 1)

                    emit_f1_batch(0)
                    warm(12, po0)
                    emit_transposes(0)
                    emit_transposes(1)
                    warm(16, po0)
                    t.wait_ge(s_tev, 1)
                    emit_block_mms(0, [0, 4], first=True, last=False)
                    emit_f1_batch(1)
                    emit_block_mms(0, [1, 5], first=False, last=False)
                    emit_f1_batch(2)
                    emit_block_mms(0, [2, 6], first=False, last=False)
                    emit_f1_batch(3)
                    emit_block_mms(0, [3, 7], first=False, last=True)
                    t.wait_ge(s_fev, 1)
                    warm(36, po0)
                    for i in range(1, NBLK):
                        if 2 <= i <= 13:
                            warm(10, ptr[(i - 1) % 2])
                        if i + 1 < NBLK:
                            emit_transposes(i + 1)
                        if i >= 2:
                            t.wait_ge(s_fev, i - 1)
                        if i in (1, 3):
                            t.wait_ge(s_wch, 4)
                        t.wait_ge(s_tev, i + 1)
                        emit_block_mms(i, CH_ORDER, first=True, last=True)

                @blk.scalar
                def _(s):
                    for k in range(8):
                        s.wait_ge(s_wmm, k + 1)
                        s.copy(f2all[:, k * 512:(k + 1) * 512],
                               po1[:, (k % 2) * 512:(k % 2) * 512 + 512])
                        s.drain().then_inc(s_wf2, 1)
                    def tev(i):
                        s.wait_ge(s_tmm, i + 1)
                        if i >= 2:
                            s.wait_ge(s_fmm, i - 1)
                        s.copy(ln2t[:, (i % 2) * 1024:(i % 2) * 1024 + 1024],
                               ptr[i % 2][:])
                        s.drain().then_inc(s_tev, 1)

                    FFN_CHUNKS = [(0, 8), (8, 8), (16, 8), (24, 8)]
                    PIDX = [2, 3, 0, 1]
                    emit_lnexpn(s, PIDX[0], 2, inv2, 0, 8)
                    LNX_AT = {5: 1, 10: 2, 14: 3}
                    for i in range(NBLK):
                        if i in LNX_AT:
                            ci = LNX_AT[i]
                            jb, nb_ = FFN_CHUNKS[ci]
                            emit_lnexpn(s, PIDX[ci], 2 + ci, inv2, jb, nb_)
                        tev(i)

                @blk.vector
                def _(v):
                    def norm2(i):
                        if i >= 2:
                            v.wait_ge(s_tmm, i - 1)
                        v.tensor_scalar(ln2b[:, (i % 2) * 1024:(i % 2) * 1024 + 1024],
                                        ux[:, i * D:(i + 1) * D],
                                        inv2[:, i:i + 1], nb2[:, i:i + 1],
                                        OP.mult, OP.add)
                        v.drain().then_inc(s_ln2, 1)

                    def stats2(j):
                        v.bn_stats(sst[:, j * 12: j * 12 + 6], ux[:, j * D: j * D + 512])
                        v.bn_stats(sst[:, j * 12 + 6: j * 12 + 12],
                                   ux[:, j * D + 512: j * D + 1024])

                    FFN_CHUNKS = [(0, 8), (8, 8), (16, 8), (24, 8)]
                    PIDX = [2, 3, 0, 1]
                    v.wait_ge(s_xev, NPAIR)
                    v.drain()
                    for j in range(8):
                        stats2(j)
                    emit_postAn(v, 0, 8, PIDX[0], 2)
                    emit_postBn(v, 0, 8, inv2, nb2, 2)
                    norm2(0)
                    norm2(1)
                    for q in range(4):
                        if q == 0:
                            v.wait_ge(s_wf2, 8)
                        for c in range(4):
                            v.wait_ge(s_wmm, 8 + 8 * q + 2 * c + 2)
                            ps0 = po1[:, 0:512]
                            ps1 = po1[:, 512:1024]
                            f2R = f2all[:, (2 * c) * 512:(2 * c) * 512 + 512]
                            f2I = f2all[:, (2 * c + 1) * 512:(2 * c + 1) * 512 + 512]
                            hR = wacc[:, 0:512]
                            hI = wacc[:, 512:1024]
                            if c == 0:
                                v.tensor_tensor(hR, ps0, f2R, OP.mult)
                                v.tensor_tensor(hI, ps1, f2I, OP.mult)
                                v.drain().then_inc(s_wfm, 2)
                            else:
                                v.tensor_tensor(pscr[:, 0:512], ps0, f2R, OP.mult)
                                v.tensor_tensor(pscr[:, 512:1024], ps1, f2I, OP.mult)
                                v.drain()
                                v.tensor_tensor(hR, hR, pscr[:, 0:512], OP.add)
                                v.tensor_tensor(hI, hI, pscr[:, 512:1024], OP.add)
                                v.drain().then_inc(s_wfm, 2)
                        v.drain()
                        v.tensor_copy(wbig[:, q * 1024: q * 1024 + 512], wacc[:, 0:512])
                        v.tensor_copy(wbig[:, q * 1024 + 512: q * 1024 + 1024], wacc[:, 512:1024])
                        v.tensor_scalar(wbig[:, (q + 4) * 1024:(q + 4) * 1024 + 512],
                                        wacc[:, 512:1024], -1.0, None, OP.mult)
                        v.tensor_copy(wbig[:, (q + 4) * 1024 + 512:(q + 4) * 1024 + 1024],
                                      wacc[:, 0:512])
                        v.drain().then_inc(s_wch, 1)
                    AGG_AT = {4: 1, 8: 2, 12: 3}
                    for i in range(NBLK):
                        if i <= 11:
                            stats2(8 + 2 * i)
                            stats2(9 + 2 * i)
                        if i in AGG_AT:
                            ci = AGG_AT[i]
                            jb, nb_ = FFN_CHUNKS[ci]
                            emit_postAn(v, jb, nb_, PIDX[ci], 2 + ci)
                            emit_postBn(v, jb, nb_, inv2, nb2, 2 + ci)
                        if i + 2 < NBLK:
                            norm2(i + 2)
                        v.wait_ge(s_fmm, i + 1)
                        if i >= 2:
                            v.wait_ge(s_st[(i - 2) // 2], 32)
                        v.tensor_tensor(outsb[:, (i % 2) * 1024:(i % 2) * 1024 + 1024],
                                        po[i % 2][:], ux[:, i * D:(i + 1) * D],
                                        OP.add)
                        v.drain().then_inc(s_fev, 1)

                @blk.gpsimd
                def _(g):
                    for i in range(NBLK):
                        g.wait_ge(s_fev, i + 1)
                        g.dma_start(out=out_hbm[128 * i:128 * (i + 1), :],
                                    in_=outsb[:, (i % 2) * 1024:(i % 2) * 1024 + 1024]
                                    ).then_inc(s_st[i // 2], 16)
                    if DEBUG_DUMPS:
                        g.wait_ge(s_wch, 4)
                        g.dma_start(out=d_wbig, in_=wbig[:]).then_inc(s_c16, 16)
                        g.dma_start(out=d_stats[:, 64:96], in_=inv2[:]).then_inc(s_c16, 16)
                        g.dma_start(out=d_stats[:, 96:128], in_=nb2[:]).then_inc(s_c16, 16)
                        g.dma_start(out=d_stats[:, 128:160], in_=nb1[:]).then_inc(s_c16, 16)
                        g.wait_ge(s_c16, 32 + 16 * 8)
                    for k in range(NBLK // 2):
                        g.wait_ge(s_st[k], 32)
    return nc


def host_prep(inputs):
    dk = np.asarray(inputs["L0_attn_diff_kernel"], np.float32)
    mk = np.asarray(inputs["L0_attn_ma_r_kernel"], np.float32)
    G1R = np.asarray(inputs["L0_ffn_G1_R"], np.float32)
    G2R = np.asarray(inputs["L0_ffn_G2_R"], np.float32)
    G1I = np.asarray(inputs["L0_ffn_G1_I"], np.float32)
    G2I = np.asarray(inputs["L0_ffn_G2_I"], np.float32)
    br = np.asarray(inputs["L0_ffn_bias_real"], np.float32)
    bi = np.asarray(inputs["L0_ffn_bias_imag"], np.float32)

    taps_ma = np.zeros((64, SEGM), np.float32)
    for i, r in enumerate(MA_ORDER):
        kr, jj = r // 8, r % 8
        row = mk[kr * 4 + (jj - 4)]
        taps_ma[i, 127:127 + row.shape[0]] = row
    taps_df = np.zeros((64, SEGD), np.float32)
    for i, r in enumerate(DF_ORDER):
        kr, jj = r // 8, r % 8
        row = dk[kr * 4 + jj]
        taps_df[i, 127:127 + row.shape[0]] = row

    atR = np.ascontiguousarray(np.transpose(G1R[0], (1, 2, 0)))
    atI = np.ascontiguousarray(np.transpose(G1I[0], (1, 2, 0)))
    btR = np.ascontiguousarray(np.transpose(G2R[:, :, 0, :], (0, 2, 1)))
    btI = np.ascontiguousarray(np.transpose(G2I[:, :, 0, :], (0, 2, 1)))

    biascat = np.concatenate([br, bi]).reshape(1, D)

    selq = np.zeros((4, 32, 128), np.float16)
    for q in range(4):
        for m in range(128):
            selq[q, 8 * q + m // 16, m] = 1.0
    selq_s = np.ascontiguousarray(np.transpose(selq, (1, 0, 2)).reshape(32, 512))
    selg = np.zeros((16, 128), np.float16)
    for m in range(128):
        selg[m % 16, m] = 1.0
    sid = np.eye(128, dtype=np.float16)
    ones = np.ones((1, 128), np.float16)
    return dict(taps_ma=np.ascontiguousarray(taps_ma.reshape(-1)),
                taps_df=np.ascontiguousarray(taps_df.reshape(-1)),
                atR=atR, atI=atI, btR=btR, btI=btI,
                biascat=np.ascontiguousarray(biascat),
                selq=selq_s, selg=selg, sid=sid, ones=ones)


def kernel(L0_attn_u, L0_attn_diff_kernel, L0_attn_ma_r_kernel,
           L0_ffn_G1_R, L0_ffn_G2_R, L0_ffn_G1_I, L0_ffn_G2_I,
           L0_ffn_bias_real, L0_ffn_bias_imag):
    inputs = dict(L0_attn_u=L0_attn_u, L0_attn_diff_kernel=L0_attn_diff_kernel,
                  L0_attn_ma_r_kernel=L0_attn_ma_r_kernel,
                  L0_ffn_G1_R=L0_ffn_G1_R, L0_ffn_G2_R=L0_ffn_G2_R,
                  L0_ffn_G1_I=L0_ffn_G1_I, L0_ffn_G2_I=L0_ffn_G2_I,
                  L0_ffn_bias_real=L0_ffn_bias_real,
                  L0_ffn_bias_imag=L0_ffn_bias_imag)
    cst = host_prep(inputs)
    u = np.asarray(L0_attn_u, np.float32)
    nc = build_nc()
    in_maps = []
    for b in range(B):
        m = {"u": np.ascontiguousarray(u[b])}
        m.update(cst)
        in_maps.append(m)
    res = run_bass_kernel_spmd(nc, in_maps, core_ids=list(range(B)))
    out = np.stack([res.results[b]["out"] for b in range(B)], axis=0)
    return out


# version 13 (final)
# speedup vs baseline: 1.0218x; 1.0218x over previous
import numpy as np
import ml_dtypes
import concourse.bass as bass
import concourse.mybir as mybir
from concourse.ap import AP
from concourse.bass_utils import run_bass_kernel_spmd

F32 = mybir.dt.float32
F16 = mybir.dt.float16
AF = mybir.ActivationFunctionType
OP = mybir.AluOpType
AX = mybir.AxisListType

B, L, D = 8, 4096, 1024
NBLK = 32
RUN = 38
NF = 128
SEGM = 1024
SEGD = 384
WFM = 1024
WFD = 384
WFRM = 896
WFRD = 256
GM = 4
GD = 8
EPS = 1e-5
NCHUNK = 4
DEBUG_DUMPS = False


NQ = 4


def _groups():
    gs = []
    for q in range(NQ):
        dfq = [r for r in range(32 * q, 32 * q + 32) if (r % 8) < 4]
        maq = [r for r in range(32 * q, 32 * q + 32) if (r % 8) >= 4]
        for i in range(0, 16, GD):
            gs.append(("d", dfq[i:i + GD]))
        for i in range(0, 16, GM):
            gs.append(("m", maq[i:i + GM]))
    return gs


GROUPS = _groups()
NGRP = len(GROUPS)
PROC = [r for _, lst in GROUPS for r in lst]
NPAIR = NF // 2
MA_ORDER = [r for r in range(NF) if (r % 8) >= 4]
DF_ORDER = [r for r in range(NF) if (r % 8) < 4]
MA_POS = {r: i for i, r in enumerate(MA_ORDER)}
DF_POS = {r: i for i, r in enumerate(DF_ORDER)}
GCUM = np.cumsum([len(lst) for _, lst in GROUPS]).tolist()
for _p in range(NPAIR):
    assert PROC[2 * _p + 1] == PROC[2 * _p] + 1


def build_nc():
    nc = bass.Bass("TRN2", target_bir_lowering=False, detect_race_conditions=False)
    u_hbm = nc.dram_tensor("u", [L, D], F32, kind="ExternalInput").ap()
    taps_ma = nc.dram_tensor("taps_ma", [64 * SEGM], F32, kind="ExternalInput").ap()
    taps_df = nc.dram_tensor("taps_df", [64 * SEGD], F32, kind="ExternalInput").ap()
    atR_h = nc.dram_tensor("atR", [4, 32, 32], F32, kind="ExternalInput").ap()
    atI_h = nc.dram_tensor("atI", [4, 32, 32], F32, kind="ExternalInput").ap()
    btR_h = nc.dram_tensor("btR", [4, 16, 16], F32, kind="ExternalInput").ap()
    btI_h = nc.dram_tensor("btI", [4, 16, 16], F32, kind="ExternalInput").ap()
    bias_h = nc.dram_tensor("biascat", [1, D], F32, kind="ExternalInput").ap()
    selq_h = nc.dram_tensor("selq", [32, 4 * 128], F16, kind="ExternalInput").ap()
    selg_h = nc.dram_tensor("selg", [16, 128], F16, kind="ExternalInput").ap()
    sid_h = nc.dram_tensor("sid", [128, 128], F16, kind="ExternalInput").ap()
    ones_h = nc.dram_tensor("ones", [1, 128], F16, kind="ExternalInput").ap()
    out_hbm = nc.dram_tensor("out", [L, D], F32, kind="ExternalOutput").ap()
    tma16 = nc.dram_tensor("tma16", [64 * SEGM], F16)
    tdf16 = nc.dram_tensor("tdf16", [64 * SEGD], F16)
    if DEBUG_DUMPS:
        d_ln1h = nc.dram_tensor("d_ln1h", [128, 8 + 512 * RUN], F16, kind="ExternalOutput").ap()
        d_kt0 = nc.dram_tensor("d_kt0", [128, GM * WFM], F16, kind="ExternalOutput").ap()
        d_stats = nc.dram_tensor("d_stats", [128, 6 * NBLK], F32, kind="ExternalOutput").ap()
        d_wbig = nc.dram_tensor("d_wbig", [128, 8 * 1024], F16, kind="ExternalOutput").ap()
        d_x1 = nc.dram_tensor("d_x1", [128, NBLK * D], F32, kind="ExternalOutput").ap()
        d_sst = nc.dram_tensor("d_sst", [128, NBLK * 12], F32, kind="ExternalOutput").ap()
        d_pp = nc.dram_tensor("d_pp", [128, 160], F32, kind="ExternalOutput").ap()

    sbt = nc.sbuf_tensor
    pst = nc.psum_tensor
    from contextlib import ExitStack
    with ExitStack() as stack:
        en = stack.enter_context
        ux = en(sbt("ux", [128, NBLK * D], F32))
        sst = en(sbt("sst", [128, NBLK * 12], F32))
        inv1 = en(sbt("inv1", [128, NBLK], F32))
        nb1 = en(sbt("nb1", [128, NBLK], F32))
        inv2 = en(sbt("inv2", [128, NBLK], F32))
        nb2 = en(sbt("nb2", [128, NBLK], F32))
        ppsc = en(sbt("ppsc", [128, 240], F32))
        satR = en(sbt("satR", [32, 128], F16))
        satI = en(sbt("satI", [32, 128], F16))
        sbtR = en(sbt("sbtR", [16, 64], F16))
        sbtI = en(sbt("sbtI", [16, 64], F16))
        sselq = en(sbt("sselq", [32, 4 * 128], F16))
        sselg = en(sbt("sselg", [16, 128], F16))
        ssid = en(sbt("ssid", [128, 128], F16))
        sones = en(sbt("sones", [1, 128], F16))
        sbias = en(sbt("sbias", [1, D], F16))
        (s_c16, s_cst, s_gld, s_ppv, s_pps, s_ppf, s_ln1,
         s_msz, s_cmm, s_xev, s_wmm, s_wf2, s_wfm, s_wch, s_ln2, s_tmm,
         s_tev, s_fmm, s_fev) = [
            en(nc.semaphore(n)) for n in
            ["s_c16", "s_cst", "s_gld", "s_ppv",
             "s_pps", "s_ppf", "s_ln1", "s_msz", "s_cmm", "s_xev", "s_wmm",
             "s_wf2", "s_wfm", "s_wch", "s_ln2", "s_tmm", "s_tev", "s_fmm",
             "s_fev"]]
        s_ul = [en(nc.semaphore(f"s_ul{k}")) for k in range(8)]
        s_ulo = [en(nc.semaphore(f"s_ulo{k}")) for k in range(8)]
        s_bcg = [en(nc.semaphore(f"s_bc{g}")) for g in range(NGRP)]
        s_skg = [en(nc.semaphore(f"s_sk{g}")) for g in range(NGRP)]
        s_st = [en(nc.semaphore(f"s_st{k}")) for k in range(NBLK // 2)]
        pbld = en(pst("pbld", [128, 1024], F32))
        with nc.Block() as blk:
            with ExitStack() as cstack:
                cen = cstack.enter_context
                ln1q0 = cen(sbt("ln1q0", [128, 38 * 256], F16))
                ln1q1 = cen(sbt("ln1q1", [128, 38 * 256], F16))
                rep0 = cen(sbt("rep0", [128, GM * SEGM], F16))
                rep1 = cen(sbt("rep1", [128, GM * SEGM], F16))
                kt0 = cen(sbt("kt0", [128, GM * WFM], F16))
                kt1 = cen(sbt("kt1", [128, GM * WFM], F16))
                pc0 = cen(pst("pc0", [128, 1024], F32))
                pc1 = cen(pst("pc1", [128, 1024], F32))
                pc2 = cen(pst("pc2", [128, 1024], F32))
                rep = [rep0, rep1]
                ln1q = [ln1q0, ln1q1]
                kt = [kt0, kt1]
                pconv = [pc0, pc1, pc2]

                def flat(t, off, dims):
                    return AP(t[:].tensor, off, dims)

                def ginfo(gi):
                    kind, lst = GROUPS[gi]
                    if kind == "d":
                        return kind, lst, SEGD, WFD, WFRD, tdf16, DF_POS[lst[0]]
                    return kind, lst, SEGM, WFM, WFRM, tma16, MA_POS[lst[0]]

                @blk.gpsimd
                def _(g):
                    for j in range(1, NBLK, 2):
                        g.dma_start(out=ux[:, j * D:(j + 1) * D],
                                    in_=u_hbm[128 * j:128 * (j + 1), :]).then_inc(s_ulo[j // 4], 16)
                    g.dma_start(out=tma16.ap(), in_=taps_ma).then_inc(s_c16, 16)
                    g.dma_start(out=tdf16.ap(), in_=taps_df).then_inc(s_c16, 16)
                    g.dma_start(out=satR[:], in_=AP(atR_h.tensor, 0, [[32, 32], [1024, 4], [1, 32]])).then_inc(s_gld, 16)
                    g.dma_start(out=satI[:], in_=AP(atI_h.tensor, 0, [[32, 32], [1024, 4], [1, 32]])).then_inc(s_gld, 16)
                    g.dma_start(out=sbtR[:], in_=AP(btR_h.tensor, 0, [[16, 16], [256, 4], [1, 16]])).then_inc(s_gld, 16)
                    g.dma_start(out=sbtI[:], in_=AP(btI_h.tensor, 0, [[16, 16], [256, 4], [1, 16]])).then_inc(s_gld, 16)
                    g.dma_start(out=sbias[:], in_=bias_h).then_inc(s_gld, 16)
                    g.wait_ge(s_c16, 32)
                    for gi in range(NGRP):
                        kind, lst, seg, wf, wfr, src16, p0 = ginfo(gi)
                        if gi == 2:
                            for k in range(8):
                                g.wait_ge(s_ul[k], 32)
                                g.wait_ge(s_ulo[k], 32)
                        if gi >= 2:
                            g.wait_ge(s_skg[gi - 2], 16)
                        n = len(lst) * seg
                        g.dma_start(out=rep[gi % 2][:, 0:n],
                                    in_=src16.ap()[None, p0 * seg: p0 * seg + n]
                                    .partition_broadcast(128)).then_inc(s_bcg[gi], 16)
                    if DEBUG_DUMPS:
                        g.wait_ge(s_ln1, 64)
                        g.dma_start(out=d_stats[:, 0:32], in_=inv1[:]).then_inc(s_c16, 16)
                        g.dma_start(out=d_stats[:, 32:64], in_=nb1[:]).then_inc(s_c16, 16)
                        g.dma_start(out=d_sst, in_=sst[:]).then_inc(s_c16, 16)
                        g.dma_start(out=d_pp, in_=ppsc[:]).then_inc(s_c16, 16)
                        g.wait_ge(s_xev, NPAIR)
                        g.dma_start(out=d_ln1h, in_=ln1h[:]).then_inc(s_c16, 16)
                        g.dma_start(out=d_kt0, in_=kt[0][:]).then_inc(s_c16, 16)
                        g.dma_start(out=d_x1, in_=ux[:]).then_inc(s_c16, 16)

                @blk.sync
                def _(sync):
                    sync.dma_start(out=sselq[:], in_=selq_h).then_inc(s_cst, 16)
                    sync.dma_start(out=sselg[:], in_=selg_h).then_inc(s_cst, 16)
                    sync.dma_start(out=ssid[:], in_=sid_h).then_inc(s_cst, 16)
                    sync.dma_start(out=sones[:], in_=ones_h).then_inc(s_cst, 16)
                    for j in range(0, NBLK, 2):
                        sync.dma_start(out=ux[:, j * D:(j + 1) * D],
                                       in_=u_hbm[128 * j:128 * (j + 1), :]).then_inc(s_ul[j // 4], 16)
                    for gi in range(NGRP):
                        kind, lst, seg, wf, wfr, src16, p0 = ginfo(gi)
                        gsz = len(lst)
                        if gi >= 2:
                            sync.wait_ge(s_cmm, GCUM[gi - 2])
                        sync.wait_ge(s_bcg[gi], 16)
                        src = flat(rep[gi % 2], 127, [[GM * SEGM - 1, 128], [seg, gsz], [1, wfr]])
                        dst = flat(kt[gi % 2], 0, [[GM * WFM, 128], [wf, gsz], [1, wfr]])
                        sync.dma_start(out=dst, in_=src).then_inc(s_skg[gi], 16)


                def emit_postAn(v, jb, nblk, pidx, seq):
                    base = jb * 12
                    meanv = flat(sst, base + 1, [[NBLK * 12, 128], [12, nblk], [3, 4]])
                    m2v = flat(sst, base + 2, [[NBLK * 12, 128], [12, nblk], [3, 4]])
                    q1 = ppsc[:, 64:64 + nblk]
                    q2 = ppsc[:, 80:80 + nblk]
                    mean = ppsc[:, 96:96 + nblk]
                    sq8 = ppsc[:, 112:112 + nblk]
                    q3 = ppsc[:, 128:128 + nblk]
                    q2e = ppsc[:, 224:224 + nblk]
                    msq = flat(ppsc, 144, [[240, 128], [4, nblk], [1, 4]])
                    ve = ppsc[:, pidx * 16: pidx * 16 + nblk]
                    v.drain()
                    v.tensor_reduce(q1, meanv, AX.X, OP.add)
                    v.tensor_tensor(msq, meanv, meanv, OP.mult)
                    v.tensor_reduce(q2, m2v, AX.X, OP.add)
                    v.drain()
                    v.tensor_reduce(q3, msq, AX.X, OP.add)
                    v.tensor_scalar(mean, q1, 0.25, None, OP.mult)
                    v.tensor_scalar(q2e, q2, 1.0, 1024.0 * EPS, OP.mult, OP.add)
                    v.drain()
                    v.tensor_tensor(sq8, mean, mean, OP.mult)
                    v.scalar_tensor_tensor(ve, q3, 256.0, q2e, OP.mult, OP.add)
                    v.drain()
                    v.scalar_tensor_tensor(ve, ve, 1.0 / 1024.0, sq8, OP.mult, OP.subtract)
                    v.drain().then_inc(s_ppv, 1)

                def emit_lnexpn(s, pidx, seq, invT, jb, nblk):
                    s.wait_ge(s_ppv, seq + 1)
                    lnt = ppsc[:, 208:208 + nblk]
                    s.activation(lnt, ppsc[:, pidx * 16: pidx * 16 + nblk], AF.Ln)
                    s.activation(invT[:, jb: jb + nblk], lnt, AF.Exp, scale=-0.5)
                    s.drain().then_inc(s_pps, 1)

                def emit_postBn(v, jb, nblk, invT, nbT, seq):
                    mean = ppsc[:, 96:96 + nblk]
                    v.wait_ge(s_pps, seq + 1)
                    v.scalar_tensor_tensor(nbT[:, jb: jb + nblk], mean, -1.0,
                                           invT[:, jb: jb + nblk],
                                           OP.mult, OP.mult)
                    v.drain().then_inc(s_ppf, 1)

                def emit_postA(v, c16, pidx):
                    base = c16 * 16 * 12
                    meanv = flat(sst, base + 1, [[NBLK * 12, 128], [12, 16], [3, 4]])
                    m2v = flat(sst, base + 2, [[NBLK * 12, 128], [12, 16], [3, 4]])
                    q1 = ppsc[:, 64:80]
                    q2 = ppsc[:, 80:96]
                    mean = ppsc[:, 96:112]
                    sq8 = ppsc[:, 112:128]
                    q3 = ppsc[:, 128:144]
                    q2e = ppsc[:, 224:240]
                    msq = flat(ppsc, 144, [[240, 128], [4, 16], [1, 4]])
                    ve = ppsc[:, pidx * 16: pidx * 16 + 16]
                    v.drain()
                    v.tensor_reduce(q1, meanv, AX.X, OP.add)
                    v.tensor_tensor(msq, meanv, meanv, OP.mult)
                    v.tensor_reduce(q2, m2v, AX.X, OP.add)
                    v.drain()
                    v.tensor_reduce(q3, msq, AX.X, OP.add)
                    v.tensor_scalar(mean, q1, 0.25, None, OP.mult)
                    v.tensor_scalar(q2e, q2, 1.0, 1024.0 * EPS, OP.mult, OP.add)
                    v.drain()
                    v.tensor_tensor(sq8, mean, mean, OP.mult)
                    v.scalar_tensor_tensor(ve, q3, 256.0, q2e, OP.mult, OP.add)
                    v.drain()
                    v.scalar_tensor_tensor(ve, ve, 1.0 / 1024.0, sq8, OP.mult, OP.subtract)
                    v.drain().then_inc(s_ppv, 1)

                def emit_lnexp(s, pidx, invT, c16):
                    s.wait_ge(s_ppv, pidx + 1)
                    lnt = ppsc[:, 208:224]
                    s.activation(lnt, ppsc[:, pidx * 16: pidx * 16 + 16], AF.Ln)
                    s.activation(invT[:, c16 * 16: c16 * 16 + 16], lnt, AF.Exp,
                                 scale=-0.5)
                    s.drain().then_inc(s_pps, 1)

                def emit_postB(v, c16, invT, nbT, pidx):
                    mean = ppsc[:, 96:112]
                    v.wait_ge(s_pps, pidx + 1)
                    v.scalar_tensor_tensor(nbT[:, c16 * 16: c16 * 16 + 16], mean, -1.0,
                                           invT[:, c16 * 16: c16 * 16 + 16],
                                           OP.mult, OP.mult)
                    v.drain().then_inc(s_ppf, 1)

                @blk.vector
                def _(v):
                    v.memset(ln1q0[:, 0:6 * 256], 0.0)
                    v.memset(ln1q1[:, 0:6 * 256], 0.0)
                    v.drain().then_inc(s_msz, 1)
                    for j in range(NBLK):
                        if j % 4 == 0:
                            v.wait_ge(s_ul[j // 4], 32)
                            v.wait_ge(s_ulo[j // 4], 32)
                        v.bn_stats(sst[:, j * 12: j * 12 + 6], ux[:, j * D: j * D + 512])
                        v.bn_stats(sst[:, j * 12 + 6: j * 12 + 12], ux[:, j * D + 512: j * D + 1024])
                        if j % 16 == 15:
                            c16 = j // 16
                            emit_postA(v, c16, c16)
                            emit_postB(v, c16, inv1, nb1, c16)
                    for p in range(NPAIR):
                        v.wait_ge(s_cmm, 2 * (p + 1))
                        r0 = PROC[2 * p]
                        psrc = flat(pconv[p % 3], 0, [[1024, 128], [512, 2], [8, 32], [1, 8]])
                        udst = flat(ux, 8 * r0, [[NBLK * D, 128], [8, 2], [D, 32], [1, 8]])
                        v.tensor_tensor(udst, psrc, udst, OP.add)
                        if p % 2 == 1:
                            v.drain().then_inc(s_xev, 2)

                @blk.scalar
                def _(s):
                    for q in range(NQ):
                        for j in range(NBLK):
                            if q == 0 and j % 16 == 0:
                                emit_lnexp(s, j // 16, inv1, j // 16)
                            if q >= 2 and j == 0:
                                s.wait_ge(s_cmm, 32 * (q - 1))
                            if q == 0:
                                s.wait_ge(s_ppf, j // 16 + 1)
                            if q == 0 and j == 0:
                                s.wait_ge(s_msz, 1)
                            dst = ln1q[q % 2][:, (6 + j) * 256:(7 + j) * 256]
                            s.activation(dst, ux[:, j * D + 256 * q: j * D + 256 * q + 256],
                                         AF.Identity, bias=nb1[:, j:j + 1],
                                         scale=inv1[:, j:j + 1])
                            if j % 16 == 15:
                                s.drain().then_inc(s_ln1, 16)

                @blk.tensor
                def _(t):
                    fidx = 0
                    for gi in range(NGRP):
                        kind, lst, seg, wf, wfr, src16, p0 = ginfo(gi)
                        q = gi // 6
                        t.wait_ge(s_skg[gi], 16)
                        t.wait_ge(s_ln1, 32 * (q + 1))
                        for sl, r in enumerate(lst):
                            pair, sub = fidx // 2, fidx % 2
                            if pair >= 3 and sub == 0:
                                t.wait_ge(s_xev, pair - 2)
                            nd = 2 if kind == "d" else 7
                            pt = pconv[pair % 3]
                            for dlt in range(nd):
                                rhs = flat(ln1q[q % 2], (6 - dlt) * 256 + (8 * r - 256 * q),
                                           [[38 * 256, 128], [256, 32], [1, 8]])
                                ins = t.matmul(
                                    pt[:, sub * 512: sub * 512 + 256],
                                    flat(kt[gi % 2], sl * wf + 128 * dlt,
                                         [[GM * WFM, 128], [1, 128]]),
                                    rhs, start=(dlt == 0), stop=(dlt == nd - 1))
                            ins.then_inc(s_cmm, 1)
                            fidx += 1

            with ExitStack() as fstack:
                fen = fstack.enter_context
                wbig = fen(sbt("wbig", [128, 8 * 1024], F16))
                f2all = fen(sbt("f2all", [128, 8 * 512], F16))
                wacc = fen(sbt("wacc", [128, 1024], F32))
                pscr = fen(sbt("pscr", [128, 1024], F32))
                ln2b = fen(sbt("ln2b", [128, 2 * 1024], F16))
                ln2t = fen(sbt("ln2t", [128, 2 * 1024], F16))
                outsb = fen(sbt("outsb", [128, 2 * 1024], F32))
                ptr0 = fen(pst("ptr0", [128, 1024], F32))
                ptr1 = fen(pst("ptr1", [128, 1024], F32))
                po0 = fen(pst("po0", [128, 1024], F32))
                po1 = pbld
                ptr = [ptr0, ptr1]
                po = [po0, po1]

                def flat2(t, off, dims):
                    return AP(t[:].tensor, off, dims)

                CH_ORDER = [0, 4, 1, 5, 2, 6, 3, 7]
                CH_GATE = {c: (c % 4) + 1 for c in range(8)}

                @blk.tensor
                def _(t):
                    def warm(n, dst):
                        for _ in range(n):
                            t.matmul(dst[:, 0:128], ssid[:], ssid[:],
                                     start=True, stop=True)

                    warm(24, pbld)
                    t.wait_ge(s_xev, NPAIR)
                    t.wait_ge(s_cst, 64)
                    t.wait_ge(s_gld, 80)
                    k = 0
                    for c in range(4):
                        for bt_ in (sbtR, sbtI):
                            if k >= 2:
                                t.wait_ge(s_wf2, k - 1)
                            rhs = flat2(bt_, c * 16, [[64, 16], [0, 32], [1, 16]])
                            t.matmul(po1[:, (k % 2) * 512:(k % 2) * 512 + 512],
                                     sselg[:], rhs, start=True, stop=True).then_inc(s_wmm, 1)
                            k += 1
                    warm(12, po0)

                    def emit_f1_batch(q):
                        kk = 8 * q
                        for c in range(4):
                            for at_ in (satR, satI):
                                m = kk + 2 * c + (0 if at_ is satR else 1)
                                if q == 0 and m == 0:
                                    t.wait_ge(s_wf2, 8)
                                if m >= 2:
                                    t.wait_ge(s_wfm, m - 1)
                                rhs = flat2(at_, c * 32, [[128, 32], [1, 32], [0, 16]])
                                t.matmul(po1[:, (m % 2) * 512:(m % 2) * 512 + 512],
                                         sselq[:, q * 128:(q + 1) * 128], rhs,
                                         start=True, stop=True).then_inc(s_wmm, 1)

                    def emit_block_mms(i, chunks, first, last):
                        for c in chunks:
                            t.wait_ge(s_wch, CH_GATE[c])
                            for hh in range(2):
                                t.matmul(po[i % 2][:, hh * 512: hh * 512 + 512],
                                         ln2t[:, (i % 2) * 1024 + c * 128:
                                              (i % 2) * 1024 + c * 128 + 128],
                                         wbig[:, c * 1024 + hh * 512: c * 1024 + hh * 512 + 512],
                                         start=(c == CH_ORDER[0] and first),
                                         stop=False)
                        if last:
                            for hh in range(2):
                                ins = t.matmul(po[i % 2][:, hh * 512: hh * 512 + 512],
                                               sones[:], sbias[:, hh * 512: hh * 512 + 512],
                                               start=False, stop=True)
                            ins.then_inc(s_fmm, 1)

                    def emit_transposes(i):
                        t.wait_ge(s_ln2, i + 1)
                        if i >= 2:
                            t.wait_ge(s_tev, i - 1)
                        for c in range(8):
                            ins = t.matmul(ptr[i % 2][:, c * 128:(c + 1) * 128],
                                           ln2b[:, (i % 2) * 1024 + c * 128:
                                                (i % 2) * 1024 + c * 128 + 128],
                                           ssid[:], start=True, stop=True)
                        ins.then_inc(s_tmm, 1)

                    emit_f1_batch(0)
                    warm(12, po0)
                    emit_transposes(0)
                    emit_transposes(1)
                    warm(16, po0)
                    t.wait_ge(s_tev, 1)
                    emit_block_mms(0, [0, 4], first=True, last=False)
                    emit_f1_batch(1)
                    emit_block_mms(0, [1, 5], first=False, last=False)
                    emit_f1_batch(2)
                    emit_block_mms(0, [2, 6], first=False, last=False)
                    emit_f1_batch(3)
                    emit_block_mms(0, [3, 7], first=False, last=True)
                    t.wait_ge(s_fev, 1)
                    warm(36, po0)
                    for i in range(1, NBLK):
                        if i + 1 < NBLK:
                            emit_transposes(i + 1)
                        if i >= 2:
                            t.wait_ge(s_fev, i - 1)
                        if i in (1, 3):
                            t.wait_ge(s_wch, 4)
                        t.wait_ge(s_tev, i + 1)
                        emit_block_mms(i, CH_ORDER, first=True, last=True)

                @blk.scalar
                def _(s):
                    for k in range(8):
                        s.wait_ge(s_wmm, k + 1)
                        s.copy(f2all[:, k * 512:(k + 1) * 512],
                               po1[:, (k % 2) * 512:(k % 2) * 512 + 512])
                        s.drain().then_inc(s_wf2, 1)
                    def tev(i):
                        s.wait_ge(s_tmm, i + 1)
                        if i >= 2:
                            s.wait_ge(s_fmm, i - 1)
                        s.copy(ln2t[:, (i % 2) * 1024:(i % 2) * 1024 + 1024],
                               ptr[i % 2][:])
                        s.drain().then_inc(s_tev, 1)

                    FFN_CHUNKS = [(0, 8), (8, 8), (16, 8), (24, 8)]
                    PIDX = [2, 3, 0, 1]
                    emit_lnexpn(s, PIDX[0], 2, inv2, 0, 8)
                    LNX_AT = {5: 1, 13: 2, 21: 3}
                    for i in range(NBLK):
                        if i in LNX_AT:
                            ci = LNX_AT[i]
                            jb, nb_ = FFN_CHUNKS[ci]
                            emit_lnexpn(s, PIDX[ci], 2 + ci, inv2, jb, nb_)
                        tev(i)

                @blk.vector
                def _(v):
                    def norm2(i):
                        if i >= 2:
                            v.wait_ge(s_tmm, i - 1)
                        v.tensor_scalar(ln2b[:, (i % 2) * 1024:(i % 2) * 1024 + 1024],
                                        ux[:, i * D:(i + 1) * D],
                                        inv2[:, i:i + 1], nb2[:, i:i + 1],
                                        OP.mult, OP.add)
                        v.drain().then_inc(s_ln2, 1)

                    def stats2(j):
                        v.bn_stats(sst[:, j * 12: j * 12 + 6], ux[:, j * D: j * D + 512])
                        v.bn_stats(sst[:, j * 12 + 6: j * 12 + 12],
                                   ux[:, j * D + 512: j * D + 1024])

                    FFN_CHUNKS = [(0, 8), (8, 8), (16, 8), (24, 8)]
                    PIDX = [2, 3, 0, 1]
                    v.wait_ge(s_xev, NPAIR)
                    v.drain()
                    for j in range(8):
                        stats2(j)
                    emit_postAn(v, 0, 8, PIDX[0], 2)
                    emit_postBn(v, 0, 8, inv2, nb2, 2)
                    norm2(0)
                    norm2(1)
                    for q in range(4):
                        if q == 0:
                            v.wait_ge(s_wf2, 8)
                        for c in range(4):
                            v.wait_ge(s_wmm, 8 + 8 * q + 2 * c + 2)
                            ps0 = po1[:, 0:512]
                            ps1 = po1[:, 512:1024]
                            f2R = f2all[:, (2 * c) * 512:(2 * c) * 512 + 512]
                            f2I = f2all[:, (2 * c + 1) * 512:(2 * c + 1) * 512 + 512]
                            hR = wacc[:, 0:512]
                            hI = wacc[:, 512:1024]
                            if c == 0:
                                v.tensor_tensor(hR, ps0, f2R, OP.mult)
                                v.tensor_tensor(hI, ps1, f2I, OP.mult)
                                v.drain().then_inc(s_wfm, 2)
                            else:
                                v.tensor_tensor(pscr[:, 0:512], ps0, f2R, OP.mult)
                                v.tensor_tensor(pscr[:, 512:1024], ps1, f2I, OP.mult)
                                v.drain()
                                v.tensor_tensor(hR, hR, pscr[:, 0:512], OP.add)
                                v.tensor_tensor(hI, hI, pscr[:, 512:1024], OP.add)
                                v.drain().then_inc(s_wfm, 2)
                        v.drain()
                        v.tensor_copy(wbig[:, q * 1024: q * 1024 + 512], wacc[:, 0:512])
                        v.tensor_copy(wbig[:, q * 1024 + 512: q * 1024 + 1024], wacc[:, 512:1024])
                        v.tensor_scalar(wbig[:, (q + 4) * 1024:(q + 4) * 1024 + 512],
                                        wacc[:, 512:1024], -1.0, None, OP.mult)
                        v.tensor_copy(wbig[:, (q + 4) * 1024 + 512:(q + 4) * 1024 + 1024],
                                      wacc[:, 0:512])
                        v.drain().then_inc(s_wch, 1)
                    AGG_AT = {4: 1, 12: 2, 20: 3}
                    STATS_AT = {}
                    for _i in range(4):
                        STATS_AT[_i] = [8 + 2 * _i, 9 + 2 * _i]
                    for _i in range(4, 20):
                        STATS_AT[_i] = [12 + _i]
                    for i in range(NBLK):
                        for _j in STATS_AT.get(i, []):
                            stats2(_j)
                        if i in AGG_AT:
                            ci = AGG_AT[i]
                            jb, nb_ = FFN_CHUNKS[ci]
                            emit_postAn(v, jb, nb_, PIDX[ci], 2 + ci)
                            emit_postBn(v, jb, nb_, inv2, nb2, 2 + ci)
                        if i + 2 < NBLK:
                            norm2(i + 2)
                        v.wait_ge(s_fmm, i + 1)
                        if i >= 2:
                            v.wait_ge(s_st[(i - 2) // 2], 32)
                        v.tensor_tensor(outsb[:, (i % 2) * 1024:(i % 2) * 1024 + 1024],
                                        po[i % 2][:], ux[:, i * D:(i + 1) * D],
                                        OP.add)
                        v.drain().then_inc(s_fev, 1)

                @blk.gpsimd
                def _(g):
                    for i in range(NBLK):
                        g.wait_ge(s_fev, i + 1)
                        g.dma_start(out=out_hbm[128 * i:128 * (i + 1), :],
                                    in_=outsb[:, (i % 2) * 1024:(i % 2) * 1024 + 1024]
                                    ).then_inc(s_st[i // 2], 16)
                    if DEBUG_DUMPS:
                        g.wait_ge(s_wch, 4)
                        g.dma_start(out=d_wbig, in_=wbig[:]).then_inc(s_c16, 16)
                        g.dma_start(out=d_stats[:, 64:96], in_=inv2[:]).then_inc(s_c16, 16)
                        g.dma_start(out=d_stats[:, 96:128], in_=nb2[:]).then_inc(s_c16, 16)
                        g.dma_start(out=d_stats[:, 128:160], in_=nb1[:]).then_inc(s_c16, 16)
                        g.wait_ge(s_c16, 32 + 16 * 8)
                    for k in range(NBLK // 2):
                        g.wait_ge(s_st[k], 32)
    return nc


def host_prep(inputs):
    dk = np.asarray(inputs["L0_attn_diff_kernel"], np.float32)
    mk = np.asarray(inputs["L0_attn_ma_r_kernel"], np.float32)
    G1R = np.asarray(inputs["L0_ffn_G1_R"], np.float32)
    G2R = np.asarray(inputs["L0_ffn_G2_R"], np.float32)
    G1I = np.asarray(inputs["L0_ffn_G1_I"], np.float32)
    G2I = np.asarray(inputs["L0_ffn_G2_I"], np.float32)
    br = np.asarray(inputs["L0_ffn_bias_real"], np.float32)
    bi = np.asarray(inputs["L0_ffn_bias_imag"], np.float32)

    taps_ma = np.zeros((64, SEGM), np.float32)
    for i, r in enumerate(MA_ORDER):
        kr, jj = r // 8, r % 8
        row = mk[kr * 4 + (jj - 4)]
        taps_ma[i, 127:127 + row.shape[0]] = row
    taps_df = np.zeros((64, SEGD), np.float32)
    for i, r in enumerate(DF_ORDER):
        kr, jj = r // 8, r % 8
        row = dk[kr * 4 + jj]
        taps_df[i, 127:127 + row.shape[0]] = row

    atR = np.ascontiguousarray(np.transpose(G1R[0], (1, 2, 0)))
    atI = np.ascontiguousarray(np.transpose(G1I[0], (1, 2, 0)))
    btR = np.ascontiguousarray(np.transpose(G2R[:, :, 0, :], (0, 2, 1)))
    btI = np.ascontiguousarray(np.transpose(G2I[:, :, 0, :], (0, 2, 1)))

    biascat = np.concatenate([br, bi]).reshape(1, D)

    selq = np.zeros((4, 32, 128), np.float16)
    for q in range(4):
        for m in range(128):
            selq[q, 8 * q + m // 16, m] = 1.0
    selq_s = np.ascontiguousarray(np.transpose(selq, (1, 0, 2)).reshape(32, 512))
    selg = np.zeros((16, 128), np.float16)
    for m in range(128):
        selg[m % 16, m] = 1.0
    sid = np.eye(128, dtype=np.float16)
    ones = np.ones((1, 128), np.float16)
    return dict(taps_ma=np.ascontiguousarray(taps_ma.reshape(-1)),
                taps_df=np.ascontiguousarray(taps_df.reshape(-1)),
                atR=atR, atI=atI, btR=btR, btI=btI,
                biascat=np.ascontiguousarray(biascat),
                selq=selq_s, selg=selg, sid=sid, ones=ones)


def kernel(L0_attn_u, L0_attn_diff_kernel, L0_attn_ma_r_kernel,
           L0_ffn_G1_R, L0_ffn_G2_R, L0_ffn_G1_I, L0_ffn_G2_I,
           L0_ffn_bias_real, L0_ffn_bias_imag):
    inputs = dict(L0_attn_u=L0_attn_u, L0_attn_diff_kernel=L0_attn_diff_kernel,
                  L0_attn_ma_r_kernel=L0_attn_ma_r_kernel,
                  L0_ffn_G1_R=L0_ffn_G1_R, L0_ffn_G2_R=L0_ffn_G2_R,
                  L0_ffn_G1_I=L0_ffn_G1_I, L0_ffn_G2_I=L0_ffn_G2_I,
                  L0_ffn_bias_real=L0_ffn_bias_real,
                  L0_ffn_bias_imag=L0_ffn_bias_imag)
    cst = host_prep(inputs)
    u = np.asarray(L0_attn_u, np.float32)
    nc = build_nc()
    in_maps = []
    for b in range(B):
        m = {"u": np.ascontiguousarray(u[b])}
        m.update(cst)
        in_maps.append(m)
    res = run_bass_kernel_spmd(nc, in_maps, core_ids=list(range(B)))
    out = np.stack([res.results[b]["out"] for b in range(B)], axis=0)
    return out
